# version 36
# baseline (speedup 1.0000x reference)
import numpy as np
import concourse.bass as bass
import concourse.mybir as mybir
from concourse.bass_utils import run_bass_kernel_spmd

F32 = mybir.dt.float32
BF16 = mybir.dt.bfloat16
I32 = mybir.dt.int32
I8 = mybir.dt.int8
U8 = mybir.dt.uint8
AF = mybir.ActivationFunctionType
ALU = mybir.AluOpType

NCORES = 8
D = 1024
TOK = 2048
NCH = 8
NG = 4
TG = 512
HCN = 24
S_DEQ = 1.0 / 127.0
S_DEQ2 = S_DEQ * S_DEQ
EPS = 1e-5

_ENGS = ("sync", "scalar", "tensor", "vector", "gpsimd")


class _Res:
    __slots__ = ("w", "readers")

    def __init__(self):
        self.w = None
        self.readers = []

    def inherit(self, *olds):
        for o in olds:
            if o.w is not None:
                self.readers.append(o.w)
            self.readers.extend(o.readers)
        return self


class _Plan:
    def __init__(self):
        self.semval = {}
        self.ops = {e: [] for e in _ENGS}
        self.waited = {}

    def _tok_waits(self, eng, toks):
        need = {}
        for t in toks:
            if t is None:
                continue
            s, v = t
            if need.get(s, 0) < v:
                need[s] = v
        out = []
        for s, v in need.items():
            if self.waited.get((eng, s), 0) < v:
                out.append((s, v))
                self.waited[(eng, s)] = v
        return out

    def op(self, eng, sem, inc, *parts):
        self.semval[sem] = self.semval.get(sem, 0) + inc
        tok = (sem, self.semval[sem])
        n = len(parts)
        for i, part in enumerate(parts):
            builder, reads, writes = part[0], part[1], part[2]
            toks = []
            for r in reads:
                toks.append(r.w)
            for w in writes:
                toks.append(w.w)
                toks.extend(w.readers)
            waits = self._tok_waits(eng, toks)
            self.ops[eng].append((builder, waits, (sem, inc) if i == n - 1 else None))
        for part in parts:
            builder, reads, writes = part[0], part[1], part[2]
            war = part[3] if len(part) > 3 else ()
            for r in reads:
                r.readers.append(tok)
            for r in war:
                r.readers.append(tok)
            for w in writes:
                w.w = tok
                w.readers = []
        return tok

    def raw_wait(self, eng, sem, val):
        if self.waited.get((eng, sem), 0) < val:
            self.ops[eng].append((None, [(sem, val)], None))
            self.waited[(eng, sem)] = val


_DMA_POOL = 12
_DMA_INFLIGHT = 6


class _DmaRR:

    def __init__(self, plan, eng, prefix):
        self.plan = plan
        self.eng = eng
        self.prefix = prefix
        self.idx = 0
        self.hist = []

    def dma(self, builder, reads, writes):
        k = self.idx
        self.idx += 1
        sem = f"{self.prefix}{k % _DMA_POOL}"
        occ = k // _DMA_POOL + 1
        if occ > 1:
            self.plan.raw_wait(self.eng, sem, 16 * (occ - 1))
        self.hist.append((sem, occ))
        return self.plan.op(self.eng, sem, 16, (builder, tuple(reads), tuple(writes)))


def _build_nc():
    nc = bass.Bass("TRN2", target_bir_lowering=False, debug=False)

    x_d = nc.dram_tensor("x", [128, NCH * TOK], F32, kind="ExternalInput").ap()
    wq_d = nc.dram_tensor("wq", [NCH, 128, 8 * 128], F32, kind="ExternalInput").ap()
    wv_d = nc.dram_tensor("wv", [NCH, 128, 8 * 128], F32, kind="ExternalInput").ap()
    wo_d = nc.dram_tensor("wo", [NCH, 128, 8 * 128], F32, kind="ExternalInput").ap()
    wgu_d = nc.dram_tensor("wgu", [HCN, 2, 128, 8 * 128], I8, kind="ExternalInput").ap()
    wd_d = nc.dram_tensor("wd", [NCH, 128, HCN * 128], I8, kind="ExternalInput").ap()
    wfeat_d = nc.dram_tensor("wfeat", [D, 6], F32, kind="ExternalInput").ap()
    biasp_d = nc.dram_tensor("biasp", [D, 3], F32, kind="ExternalInput").ap()
    y_d = nc.dram_tensor("y", [D, TOK], F32, kind="ExternalOutput").ap()
    wos_d = nc.dram_tensor("wos", [NCH, 128, 8 * 128], BF16, kind="Internal").ap()

    AB = 210944
    _resv = nc.alloc_sbuf_tensor("resv", [128, AB], U8)
    BASE = nc.lookup_mloc(_resv).addr
    _reg32 = []
    _reg16 = []

    def _sb(name, off, nelems, dtype, reg, esz):
        h = nc.alloc_sbuf_tensor_at(name, [128, nelems], dtype, offset=BASE + off)
        reg.append((off, off + nelems * esz, h.ap()))
        return h

    def _route(reg, off, nbytes):
        best = None
        for a, b, ap_ in reg:
            if a <= off and off + nbytes <= b:
                if best is None or (b - a) < (best[1] - best[0]):
                    best = (a, b, ap_)
        assert best is not None, (off, nbytes)
        return best

    def v32(off, n):
        a, b, ap_ = _route(_reg32, off, 4 * n)
        return ap_[:, (off - a) // 4: (off - a) // 4 + n]

    def v16(off, n):
        a, b, ap_ = _route(_reg16, off, 2 * n)
        return ap_[:, (off - a) // 2: (off - a) // 2 + n]

    o_consts = 0
    o_ones = 64
    o_bias = 320
    o_wfeat = 448
    o_alpha = 640
    o_msb = 768
    o_var = 2816
    o_r = 4864
    o_mr = 8960
    o_rbf = 13056
    o_mrbf = 15104
    o_ntmp = 17152
    o_sg = 21248
    o_X = 25600
    o_mix = 91136
    S0 = 123904
    o_xbf = S0
    o_qt = S0 + 32768
    o_vt = S0 + 36864
    o_s1 = S0 + 40960
    o_d1 = S0 + 45056
    o_s2 = S0 + 49152
    o_d2 = S0 + 51200
    o_s3 = S0 + 53248
    o_d3 = S0 + 54272
    o_wstbf = S0 + 55296
    o_wox = S0 + 32768
    o_ybf = S0 + 45056
    o_ysqA = S0 + 61440
    o_ysqB = S0 + 81920
    o_x1bf = S0
    o_etmp = S0 + 65536
    o_h = o_X
    o_y2 = o_X + 49152
    o_wgux = S0 + 32768
    o_wdbf = o_mix
    o_y2b = S0 + 65536
    o_y2bf = o_ybf

    for nm, off, ne in (
        ("t_cons", o_consts, 16), ("t_bias", o_bias, 24), ("t_wfeat", o_wfeat, 48),
        ("t_alpha", o_alpha, 32), ("t_msb", o_msb, 512), ("t_var", o_var, 512),
        ("t_rr", o_r, 1024), ("t_mr", o_mr, 1024), ("t_ntmp32", o_ntmp, 1024),
        ("t_sg", o_sg, 1024), ("t_X", o_X, 16384),
        ("t_fsc", o_etmp, 4096),
        ("t_dtmp", o_mix + 24576, 2048),
    ):
        _sb(nm, off, ne, F32, _reg32, 4)
    for nm, off, ne in (
        ("t_ones", o_ones, 128), ("t_rbf", o_rbf, 1024), ("t_mrbf", o_mrbf, 1024),
        ("t_ntmpbf", o_ntmp, 2048), ("t_mix", o_mix, 16384), ("t_h", o_X, 24576),
        ("t_sc16", S0, (209920 - S0) // 2),
    ):
        _sb(nm, off, ne, BF16, _reg16, 2)

    pb = [nc.alloc_psum_tensor(f"pb{i}", [128, 512], F32).ap() for i in range(8)]

    sems = {}
    for s in (["out", "pe", "act", "dve", "gp"]
              + [f"dq{i}" for i in range(_DMA_POOL)]
              + [f"gq{i}" for i in range(_DMA_POOL)]
              + [f"aq{i}" for i in range(_DMA_POOL)]):
        cm = nc.semaphore(s)
        sems[s] = cm.__enter__()

    P = _Plan()
    DQ = _DmaRR(P, "sync", "dq")
    GQ = _DmaRR(P, "gpsimd", "gq")
    AQ = _DmaRR(P, "scalar", "aq")

    rXg = [_Res() for _ in range(NG)]
    rXc = [_Res() for _ in range(NCH)]
    rxbf = [_Res() for _ in range(NCH)]
    r_mixc = [_Res() for _ in range(NCH)]
    r_qt = _Res()
    r_vt = _Res()
    r_cqp = [_Res(), _Res()]
    r_cvp = [_Res(), _Res()]
    r_wst = _Res()
    r_wos = [_Res() for _ in range(NCH)]
    r_wstbf = [_Res() for _ in range(4)]
    r_consts = _Res()
    r_ones = _Res()
    r_bias = _Res()
    r_wfeat = _Res()
    r_alpha = _Res()
    r_pb = [_Res() for _ in range(8)]
    r_msb = [_Res(), _Res()]
    r_var = _Res()
    r_r = [_Res(), _Res()]
    r_mr = [_Res(), _Res()]
    r_ntmp = [_Res(), _Res()]
    r_rbf = [_Res(), _Res()]
    r_mrbf = [_Res(), _Res()]
    r_sg = [_Res(), _Res()]

    cons = v32(o_consts, 16)
    ones_ap = v16(o_ones, 128)

    def ms(ap_, val, res):
        P.op("gpsimd", "gp", 1, (lambda e, a=ap_, v=val: e.memset(a, v), (), (res,)))

    def tt(out, i0, i1, op, reads, writes):
        P.op("vector", "dve", 1,
             (lambda e, out=out, i0=i0, i1=i1, op=op: e.tensor_tensor(out=out, in0=i0, in1=i1, op=op),
              tuple(reads), tuple(writes)))

    def tt_g(out, i0, i1, op, reads, writes):
        P.op("gpsimd", "gp", 1,
             (lambda e, out=out, i0=i0, i1=i1, op=op: e.tensor_tensor(out=out, in0=i0, in1=i1, op=op),
              tuple(reads), tuple(writes)))

    def stt_g(out, i0, sc, i1, op0, op1, reads, writes):
        assert op0 == ALU.mult and op1 == ALU.mult
        P.op("gpsimd", "gp", 1,
             (lambda e, out=out, i0=i0, i1=i1: e.tensor_tensor(out=out, in0=i0, in1=i1, op=ALU.mult),
              tuple(reads), tuple(writes)))
        P.op("vector", "dve", 1,
             (lambda e, out=out, sc=sc: e.tensor_single_scalar(out=out, in_=out, scalar=sc, op=ALU.mult),
              tuple(writes), tuple(writes)))

    def stt(out, i0, sc, i1, op0, op1, reads, writes):
        P.op("vector", "dve", 1,
             (lambda e, out=out, i0=i0, sc=sc, i1=i1, op0=op0, op1=op1: e.scalar_tensor_tensor(
                 out=out, in0=i0, scalar=sc, in1=i1, op0=op0, op1=op1),
              tuple(reads), tuple(writes)))

    ms(cons[:, 0:1], EPS, r_consts)
    ms(cons[:, 5:6], 0.5, r_consts)
    ms(ones_ap, 1.0 / 1024.0, r_ones)
    P.op("scalar", "act", 1,
         (lambda e: e.activation(out=cons[:, 5:6], in_=cons[:, 5:6], func=AF.Sqrt,
                                 bias=0.0, scale=1.0), (r_consts,), (r_consts,)))
    tt(cons[:, 2:3], cons[:, 5:6], cons[:, 5:6], ALU.mult, [r_consts], [r_consts])
    tt(cons[:, 2:3], cons[:, 2:3], cons[:, 5:6], ALU.mult, [r_consts], [r_consts])
    tt(cons[:, 4:5], cons[:, 2:3], cons[:, 2:3], ALU.mult, [r_consts], [r_consts])
    tt(cons[:, 3:4], cons[:, 4:5], cons[:, 2:3], ALU.mult, [r_consts], [r_consts])

    X3 = v32(o_X, NCH * TOK).rearrange("p (c t) -> p c t", t=TOK)
    Xflat = v32(o_X, NCH * TOK)
    DQ.dma(lambda e: e.dma_start(out=v32(o_wfeat, 48).rearrange("p (c k) -> p c k", k=6),
                                 in_=wfeat_d.rearrange("(c p) k -> p c k", p=128)),
           (), (r_wfeat,))
    DQ.dma(lambda e: e.dma_start(out=v32(o_bias, 24).rearrange("p (c k) -> p c k", k=3),
                                 in_=biasp_d.rearrange("(c p) k -> p c k", p=128)),
           (), (r_bias,))
    rXdq, rXgq = _Res(), _Res()
    xbf = v16(o_xbf, NCH * TOK)

    def _load_x_and_cast():
        for c in range(NCH):
            Q = DQ if c < 4 else GQ
            wr = (rXc[c],) + ((rXdq,) if c == 3 else (rXgq,) if c == 7 else ())
            Q.dma(lambda e, c=c: e.dma_start(out=Xflat[:, c * TOK:(c + 1) * TOK],
                                             in_=x_d[:, c * TOK:(c + 1) * TOK]),
                  (), wr)
        for c in range(NCH):
            if c % 2 == 0:
                P.op("scalar", "act", 1,
                     (lambda e, c=c: e.activation(out=xbf[:, c * TOK:(c + 1) * TOK],
                                                  in_=X3[:, c, :], func=AF.Identity,
                                                  bias=0.0, scale=1.0),
                      (rXc[c],), (rxbf[c],), tuple(rXg)))
            else:
                P.op("vector", "dve", 1,
                     (lambda e, c=c: e.tensor_copy(xbf[:, c * TOK:(c + 1) * TOK],
                                                   X3[:, c, :]),
                      (rXc[c],), (rxbf[c],), tuple(rXg)))
        P.op("vector", "dve", 1,
             (lambda e: e.tensor_copy(cons[:, 15:16], cons[:, 5:6]),
              (rXdq, rXgq, r_consts), tuple(rXg)))

    wf3 = v32(o_wfeat, 48).rearrange("p (c k) -> p c k", k=6)
    al3 = v32(o_alpha, 32).rearrange("p (c k) -> p c k", k=4)
    for j, scol in ((0, 3), (1, 3), (2, 4)):
        P.op("vector", "dve", 1,
             (lambda e, j=j, scol=scol: e.scalar_tensor_tensor(
                 out=al3[:, :, j], in0=wf3[:, :, j], scalar=cons[:, scol:scol + 1],
                 in1=wf3[:, :, 3 + j], op0=ALU.mult, op1=ALU.mult),
              (r_wfeat, r_consts), (r_alpha,)))

    qt = v16(o_qt, TOK)
    vt = v16(o_vt, TOK)
    mix = v16(o_mix, NCH * TOK)
    bias3 = v32(o_bias, 24).rearrange("p (c k) -> p c k", k=3)

    wst_slot = [0]

    def stream_slab(src_ap):
        slot = wst_slot[0] % 4
        wst_slot[0] += 1
        GQ.dma(lambda e, src=src_ap, p=slot: e.dma_start(
                   out=v16(o_wstbf + 2048 * p, 1024), in_=src),
               (), (r_wstbf[slot],))
        return v16(o_wstbf + 2048 * slot, 1024), r_wstbf[slot]

    bank_rr = [0]

    def proj_chunk(c, wbf, wres, dst_ap, dst_res, bcol, rhs_ap, rhs_res):
        for g in range(NG):
            bank = bank_rr[0] % 4
            bank_rr[0] += 1
            parts = []
            for k in range(8):
                parts.append((
                    lambda e, bank=bank, k=k, g=g, wbf=wbf: e.matmul(
                        pb[bank], wbf[:, 128 * k:128 * (k + 1)],
                        rhs_ap[:, k * TOK + g * TG: k * TOK + (g + 1) * TG],
                        start=(k == 0), stop=(k == 7)),
                    (wres, rhs_res[k]), (r_pb[bank],)))
            P.op("tensor", "pe", 1, *parts)
            P.op("scalar", "act", 1,
                 (lambda e, bank=bank, g=g, c=c, dst_ap=dst_ap, bcol=bcol: e.activation(
                     out=dst_ap[:, g * TG:(g + 1) * TG], in_=pb[bank],
                     func=AF.Identity, bias=bias3[:, c, bcol:bcol + 1], scale=1.0),
                  (r_pb[bank], r_bias), (dst_res,)))

    def _halves(ap_, blk):
        v = ap_.rearrange("p (b i) -> p b i", i=blk)
        return v[:, :, 0:blk // 2], v[:, :, blk // 2:blk]

    def mixer(c):
        qv = v16(o_qt, 2 * TOK)
        s1, d1 = v16(o_s1, 2048), v16(o_d1, 2048)
        s2, d2 = v16(o_s2, 1024), v16(o_d2, 1024)
        s3, d3 = v16(o_s3, 512), v16(o_d3, 512)
        d1q, d1v = d1[:, 0:1024], d1[:, 1024:2048]
        d2q, d2v = d2[:, 0:512], d2[:, 512:1024]
        s3q, s3v = s3[:, 0:256], s3[:, 256:512]
        d3q, d3v = d3[:, 0:256], d3[:, 256:512]
        rc = r_cqp[0]
        e, o = _halves(qv, 8)
        tt(s1.rearrange("p (b i) -> p b i", i=4), e, o, ALU.add, [r_qt, r_vt], [rc])
        tt(d1.rearrange("p (b i) -> p b i", i=4), e, o, ALU.subtract, [r_qt, r_vt], [rc])
        stt(d1q, d1q, cons[:, 2:3], d1v, ALU.mult, ALU.mult, [rc, r_consts], [rc])
        e, o = _halves(s1, 4)
        tt(s2.rearrange("p (b i) -> p b i", i=2), e, o, ALU.add, [rc], [rc])
        tt(d2.rearrange("p (b i) -> p b i", i=2), e, o, ALU.subtract, [rc], [rc])
        stt(d2q, d2q, al3[:, c, 2:3], d2v, ALU.mult, ALU.mult, [rc, r_alpha], [rc])
        e, o = _halves(s2, 2)
        tt(s3.rearrange("p (b i) -> p b i", i=1), e, o, ALU.add, [rc], [rc])
        tt(d3.rearrange("p (b i) -> p b i", i=1), e, o, ALU.subtract, [rc], [rc])
        stt(s3q, s3q, al3[:, c, 0:1], s3v, ALU.mult, ALU.mult, [rc, r_alpha], [rc])
        stt(d3q, d3q, al3[:, c, 1:2], d3v, ALU.mult, ALU.mult, [rc, r_alpha], [rc])
        a2 = s2[:, 512:1024]
        a2e, a2o = _halves(a2, 2)
        tt(a2e, s3q.rearrange("p (b i) -> p b i", i=1), d3q.rearrange("p (b i) -> p b i", i=1), ALU.add, [rc], [rc])
        tt(a2o, s3q.rearrange("p (b i) -> p b i", i=1), d3q.rearrange("p (b i) -> p b i", i=1), ALU.subtract, [rc], [rc])
        a1 = s1[:, 0:1024]
        a1e, a1o = _halves(a1, 4)
        tt(a1e, a2.rearrange("p (b i) -> p b i", i=2), d2q.rearrange("p (b i) -> p b i", i=2), ALU.add, [rc], [rc])
        tt(a1o, a2.rearrange("p (b i) -> p b i", i=2), d2q.rearrange("p (b i) -> p b i", i=2), ALU.subtract, [rc], [rc])
        mslice = mix[:, c * TOK:(c + 1) * TOK]
        me, mo = _halves(mslice, 8)
        tt(me, a1.rearrange("p (b i) -> p b i", i=4), d1q.rearrange("p (b i) -> p b i", i=4), ALU.add, [rc], [r_mixc[c]])
        tt(mo, a1.rearrange("p (b i) -> p b i", i=4), d1q.rearrange("p (b i) -> p b i", i=4), ALU.subtract, [rc], [r_mixc[c]])

    def proj_chunk0(w0q, w0v):
        streams = ((w0q[0], w0q[1], qt, r_qt, 0), (w0v[0], w0v[1], vt, r_vt, 1))
        for k in range(7):
            for si in range(2):
                wap, wres = streams[si][0], streams[si][1]
                for g in range(NG):
                    bank = 4 * si + g
                    P.op("tensor", "pe", 1,
                         (lambda e, bank=bank, k=k, g=g, wap=wap: e.matmul(
                             pb[bank], wap[:, 128 * k:128 * (k + 1)],
                             xbf[:, k * TOK + g * TG: k * TOK + (g + 1) * TG],
                             start=(k == 0), stop=False),
                          (wres, rxbf[k]), (r_pb[bank],)))
        for si in range(2):
            wap, wres, dst, dres, bcol = streams[si]
            for g in range(NG):
                bank = 4 * si + g
                P.op("tensor", "pe", 1,
                     (lambda e, bank=bank, wap=wap, g=g: e.matmul(
                         pb[bank], wap[:, 896:1024],
                         xbf[:, 7 * TOK + g * TG: 7 * TOK + (g + 1) * TG],
                         start=False, stop=True),
                      (wres, rxbf[7]), (r_pb[bank],)))
                P.op("scalar", "act", 1,
                     (lambda e, bank=bank, g=g, dst=dst, bcol=bcol: e.activation(
                         out=dst[:, g * TG:(g + 1) * TG], in_=pb[bank],
                         func=AF.Identity, bias=bias3[:, 0, bcol:bcol + 1],
                         scale=1.0),
                      (r_pb[bank], r_bias), (dres,)))
        bank_rr[0] = 0

    w0q = stream_slab(wq_d[0])
    w0v = stream_slab(wv_d[0])
    _load_x_and_cast()

    for c in range(NCH):
        if c == 0:
            proj_chunk0(w0q, w0v)
        else:
            wap, wres = stream_slab(wq_d[c])
            proj_chunk(c, wap, wres, qt, r_qt, 0, xbf, rxbf)
            wap, wres = stream_slab(wv_d[c])
            proj_chunk(c, wap, wres, vt, r_vt, 1, xbf, rxbf)
        woap, wores = stream_slab(wo_d[c])
        DQ.dma(lambda e, c=c, woap=woap: e.dma_start(out=wos_d[c], in_=woap),
               (wores,), (r_wos[c],))
        mixer(c)

    x1bf = v16(o_x1bf, NCH * TOK)
    rx1g = [[_Res().inherit(*rxbf) for _ in range(NCH)] for _ in range(NG)]


    wo_slot = [0]
    r_woslot = [_Res().inherit(r_qt), _Res().inherit(r_qt),
                _Res().inherit(r_vt), _Res().inherit(r_vt),
                _Res().inherit(*r_cqp), _Res().inherit(*r_cqp)]
    r_ybfp = [_Res().inherit(*r_cqp, *r_cvp), _Res().inherit(*r_cqp, *r_cvp)]
    r_ysqc = [_Res().inherit(r_wstbf[3]), _Res().inherit(r_wstbf[3])] \
        + [_Res() for _ in range(6)]
    E_OFF = [S0 + 77824, S0 + 79872, o_wstbf, o_wstbf + 2048, o_wstbf + 4096]
    r_wosl_e = [_Res(), _Res(),
                _Res().inherit(r_wstbf[0]), _Res().inherit(r_wstbf[1]),
                _Res().inherit(r_wstbf[2])]

    def _sq_ap(c):
        if c < 4:
            return v16(o_ysqA + 1024 * c, TG)
        return v16(o_ysqB + 1024 * (c - 4), TG)

    def _wo_slot_ap(p):
        return v16(o_wox + 2048 * p, 1024)

    def wout_slab(g, oc):
        idx = wo_slot[0]
        wo_slot[0] += 1
        if idx < 5:
            ap_, res = v16(E_OFF[idx], 1024), r_wosl_e[idx]
        else:
            slot = (idx - 5) % 8
            if slot < 6:
                ap_, res = _wo_slot_ap(slot), r_woslot[slot]
            else:
                ap_, res = v16(E_OFF[slot - 6], 1024), r_wosl_e[slot - 6]
        DQ.dma(lambda e, oc=oc, ap_=ap_: e.dma_start(out=ap_, in_=wos_d[oc]),
               (r_wos[oc],), (res,))
        return ap_, res

    o_etmp = o_y2b
    r_etmp = [_Res().inherit(*r_cvp) for _ in range(6)]
    etmp_rr = [0]

    WBANKS = [0, 1, 2, 3, 6, 7]
    wout_rr = [0]

    def _wout_mm(g, oc, bank, wbf, wres, ks, start, stop):
        parts = []
        for k in ks:
            parts.append((
                lambda e, bank=bank, k=k, g=g, wbf=wbf, s=start, st=stop, ks=ks: e.matmul(
                    pb[bank], wbf[:, 128 * k:128 * (k + 1)],
                    mix[:, k * TOK + g * TG: k * TOK + (g + 1) * TG],
                    start=(s and k == ks[0]), stop=(st and k == ks[-1])),
                (wres, r_mixc[k]), (r_pb[bank],)))
        P.op("tensor", "pe", 1, *parts)

    def _wout_tail(g, oc, bank, ybfp):
        es_ = etmp_rr[0] % 6
        etmp_rr[0] += 1
        tmp = v32(o_etmp + 2048 * es_, TG)
        P.op("scalar", "act", 1,
             (lambda e, bank=bank, tmp=tmp, oc=oc: e.activation(
                 out=tmp, in_=pb[bank], func=AF.Identity,
                 bias=bias3[:, oc, 2:3], scale=1.0),
              (r_pb[bank], r_bias), (r_etmp[es_],)))
        xsl = v32(o_X + (oc * TOK + g * TG) * 4, TG)
        ybf_w = [r_ybfp[g % 2]] + (r_wosl_e[2:] if g == 1 else [])
        tt(ybfp[:, oc * TG:(oc + 1) * TG], tmp, xsl, ALU.add,
           [r_etmp[es_], rXg[g]], ybf_w)
        pending_sqs.append((oc, ybfp, g % 2))

    PBANKS = [0, 1, 2, 3, 6, 7, 4, 5]

    def wout_group(g, mid_cb=None):
        ybfp = v16(o_ybf + 8192 * (g % 2), NCH * TG)
        if g == 0:
            info = {}
            for oc in range(NCH):
                wbf, wres = wout_slab(g, oc)
                bank = PBANKS[oc]
                info[oc] = (bank, wbf, wres)
                _wout_mm(g, oc, bank, wbf, wres, list(range(7)), True, False)
            for oc in range(NCH):
                bank, wbf, wres = info[oc]
                _wout_mm(g, oc, bank, wbf, wres, [7], False, True)
                _wout_tail(g, oc, bank, ybfp)
            wout_rr[0] = 0
        else:
            for oc in range(NCH):
                wbf, wres = wout_slab(g, oc)
                bank = WBANKS[wout_rr[0] % 6]
                wout_rr[0] += 1
                _wout_mm(g, oc, bank, wbf, wres, list(range(8)), True, True)
                if oc < 4:
                    step_wout_norm()
                _wout_tail(g, oc, bank, ybfp)

    ln_par = [0]

    def ln_group(src3, src_res, bfap, bfres, outs, bf_norm=False,
                 defer_norm=False, mb=(4, 5), pre_bf=False):
        if not pre_bf:
            P.op("scalar", "act", 1,
                 (lambda e, src3=src3: e.activation(
                     out=bfap.rearrange("p (c t) -> p c t", t=TG), in_=src3,
                     func=AF.Identity, bias=0.0, scale=1.0),
                  (src_res,), (bfres,)))
            for c in range(NCH):
                tt(_sq_ap(c), bfap[:, c * TG:(c + 1) * TG],
                   bfap[:, c * TG:(c + 1) * TG],
                   ALU.mult, [bfres], [r_ysqc[c]])
        bm, bs = mb
        parts = []
        for c in range(NCH):
            parts.append((lambda e, c=c, bm=bm: e.matmul(pb[bm], ones_ap, bfap[:, c * TG:(c + 1) * TG],
                                                         start=(c == 0), stop=(c == NCH - 1)),
                          (r_ones, bfres), (r_pb[bm],)))
        P.op("tensor", "pe", 1, *parts)
        parts = []
        for c in range(NCH):
            parts.append((lambda e, c=c, bs=bs: e.matmul(
                              pb[bs], ones_ap, _sq_ap(c),
                              start=(c == 0), stop=(c == NCH - 1)),
                          (r_ones, r_ysqc[c]), (r_pb[bs],)))
        P.op("tensor", "pe", 1, *parts)
        par = ln_par[0] % 2
        ln_par[0] += 1
        msb, var = v32(o_msb if par == 0 else o_mr, TG), v32(o_var, TG)
        rr = v32(o_r + 2048 * par, TG)
        P.op("scalar", "act", 1,
             (lambda e, bm=bm, msb=msb: e.activation(out=msb, in_=pb[bm], func=AF.Identity, bias=0.0, scale=1.0),
              (r_pb[bm],), (r_msb[par],)))
        tt(var, msb, msb, ALU.mult, [r_msb[par]], [r_var])
        tt(var, pb[bs], var, ALU.subtract, [r_pb[bs], r_var], [r_var])
        P.op("scalar", "act", 1,
             (lambda e, rr=rr: e.activation(out=rr, in_=var, func=AF.Ln, bias=cons[:, 0:1], scale=1.0),
              (r_var, r_consts), (r_r[par],)))
        P.op("scalar", "act", 1,
             (lambda e, rr=rr: e.activation(out=rr, in_=rr, func=AF.Exp, bias=0.0, scale=-0.5),
              (r_r[par],), (r_r[par],)))
        def _norm(lo=0, hi=NCH, phase=0, bf_norm=bf_norm, outs=outs,
                  src_res=src_res, bfap=bfap, bfres=bfres, rr=rr, msb=msb,
                  par=par):
            _emit_norm(bf_norm, outs, src_res, bfap, bfres, rr, msb, par,
                       lo, hi, phase)
        if defer_norm:
            return _norm
        _norm()
        return None

    def _emit_norm(bf_norm, outs, src_res, bfap, bfres, rr, msb, par,
                   lo=0, hi=NCH, phase=0):
        if bf_norm:
            rrb = v16(o_rbf + 1024 * par, TG)
            msbb = v16(o_mrbf + 1024 * par, TG)
            if lo == 0:
                P.op("vector", "dve", 1,
                     (lambda e, rrb=rrb, rr=rr: e.tensor_copy(rrb, rr), (r_r[par],), (r_rbf[par],)))
                P.op("vector", "dve", 1,
                     (lambda e, msbb=msbb, msb=msb: e.tensor_copy(msbb, msb), (r_msb[par],), (r_mrbf[par],)))
            for i in range(lo, hi):
                out_ap, c, out_res = outs[i]
                bfin = bfap[:, c * TG:(c + 1) * TG]
                ntmp = v16(o_ntmp + 2048 * (i % 2), TG)
                tt(ntmp, bfin, msbb, ALU.subtract, [bfres, r_mrbf[par]], [r_ntmp[i % 2]])
                tt(out_ap, ntmp, rrb, ALU.mult, [r_ntmp[i % 2], r_rbf[par]], [out_res])
        else:
            for i in range(lo, hi):
                out_ap, in_ap, out_res = outs[i]
                if phase != 2:
                    tt(out_ap, in_ap, msb, ALU.subtract,
                       [out_res, r_msb[par]], [out_res])
                if phase != 1:
                    tt(out_ap, out_ap, rr, ALU.mult,
                       [out_res, r_r[par]], [out_res])

    def ln1_group(g):
        outs = []
        for oc in range(NCH):
            out_ap = x1bf[:, oc * TOK + g * TG: oc * TOK + (g + 1) * TG]
            outs.append((out_ap, oc, rx1g[g][oc]))
        ybfp = v16(o_ybf + 8192 * (g % 2), NCH * TG)
        return ln_group(None, rXg[g], ybfp, r_ybfp[g % 2], outs, bf_norm=True,
                        defer_norm=True, pre_bf=True)

    pending_norm = []
    wout_norm_stages = []
    pending_sqs = []

    def step_wout_norm():
        if wout_norm_stages:
            wout_norm_stages.pop(0)()

    def flush_norm():
        while pending_norm:
            pending_norm.pop(0)()

    prev_g = [None]

    def _mid_stats():
        if prev_g[0] is not None:
            pending_norm.append(ln1_group(prev_g[0]))
            prev_g[0] = None

    def _flush_sqs():
        while pending_sqs:
            oc, ybfp_, gpar = pending_sqs.pop(0)
            tt(_sq_ap(oc), ybfp_[:, oc * TG:(oc + 1) * TG],
               ybfp_[:, oc * TG:(oc + 1) * TG], ALU.mult, [r_ybfp[gpar]],
               [r_ysqc[oc]])

    for g in range(NG):
        wout_group(g)
        if prev_g[0] is not None:
            fn = ln1_group(prev_g[0])
            if g < 3:
                for lo in range(0, NCH, 2):
                    wout_norm_stages.append(lambda fn=fn, lo=lo: fn(lo, lo + 2))
            else:
                pending_norm.append(fn)
        _flush_sqs()
        prev_g[0] = g
    while wout_norm_stages:
        wout_norm_stages.pop(0)()


    h = v16(o_h, HCN * 1024)
    r_h = _Res().inherit(*rXg)
    y2off = [o_y2, o_y2b]
    r_y2p = [[_Res().inherit(*rXg) for _ in range(4)],
             [_Res().inherit(*r_cvp, *r_cqp, *r_wstbf, *r_etmp, *r_wosl_e[:2])
              for _ in range(4)]]
    r_wgubf = [_Res().inherit(*r_woslot) for _ in range(3)]

    def _wgu_slot_ap(p):
        return v16(o_wgux + 4096 * p, 2048)
    wgu_slot = [0]
    r_wdbf = [_Res().inherit(*r_mixc) for _ in range(4)]
    y2bf = v16(o_y2bf, NCH * TG)

    store_toks = []
    wd_slot = [0]
    dwn_rr = [0]
    o_dtmp = o_mix + 24576
    r_dtmp = [_Res().inherit(*r_mixc) for _ in range(4)]
    dtmp_rr = [0]

    def ln2_stages(g):
        pg = g % 2
        ybase = y2off[pg]
        outs = []
        for oc in range(NCH):
            ap_ = v32(ybase + oc * TG * 4, TG)
            outs.append((ap_, ap_, r_y2p[pg][oc // 2]))
        y2bfp = v16(o_ybf + 8192 * pg, NCH * TG)
        box = [None]

        def s_stats():
            box[0] = ln_group(None, None, y2bfp, r_ybfp[pg], outs,
                              pre_bf=True, defer_norm=True)

        def mk_norm(lo, phase=0):
            return lambda: box[0](lo, lo + 2, phase)

        def mk_store(lo, w, Q=None):
            def s_store():
                tok = (Q or DQ).dma(lambda e, g=g, ybase=ybase, lo=lo, w=w: e.dma_start(
                         out=y_d.rearrange("(c p) t -> p c t", p=128)[
                             :, lo:lo + w, g * TG:(g + 1) * TG],
                         in_=v32(ybase + lo * TG * 4, w * TG).rearrange(
                             "p (c t) -> p c t", t=TG)),
                      (r_y2p[pg][lo // 2],), ())
                store_toks.append(tok)
            return s_store

        st = [s_stats]
        if g == 3:
            for lo in range(0, NCH, 2):
                st.append(mk_norm(lo, 1))
            for lo in range(0, NCH, 2):
                st.append(mk_norm(lo, 2))
                st.append(mk_store(lo, 1, DQ))
                st.append(mk_store(lo + 1, 1, GQ))
        else:
            for lo in range(0, NCH, 2):
                st.append(mk_norm(lo))
                st.append(mk_store(lo, 2))
        return st

    pending_stages = []

    def flush_ln2():
        while pending_stages:
            pending_stages.pop(0)()

    def step_ln2():
        if pending_stages:
            pending_stages.pop(0)()

    for hh in range(2):
        for hc in range(HCN):
            par = wgu_slot[0] % 3
            wgu_slot[0] += 1
            GQ.dma(lambda e, hc=hc, par=par: e.dma_start(
                       out=_wgu_slot_ap(par).rearrange("p (w f) -> p w f", f=1024),
                       in_=wgu_d[hc].rearrange("w p f -> p w f")),
                   (), (r_wgubf[par],))
            wgubf_ap = _wgu_slot_ap(par)
            if hc == 1:
                _mid_stats()
                while pending_norm:
                    fn = pending_norm.pop(0)
                    for lo in range(0, NCH, 2):
                        pending_stages.append(lambda fn=fn, lo=lo: fn(lo, lo + 2))
            if hc >= 1:
                step_ln2()
            for gi in range(2):
                g = 2 * hh + gi
                pidx = (2 * hc + gi) % 2
                gbank, ubank = 2 * pidx, 2 * pidx + 1
                for wsel, bank in ((0, gbank), (1, ubank)):
                    parts = []
                    for k in range(8):
                        parts.append((
                            lambda e, bank=bank, wsel=wsel, k=k, g=g, wgubf_ap=wgubf_ap: e.matmul(
                                pb[bank],
                                wgubf_ap[:, wsel * 1024 + 128 * k: wsel * 1024 + 128 * (k + 1)],
                                x1bf[:, k * TOK + g * TG: k * TOK + (g + 1) * TG],
                                start=(k == 0), stop=(k == 7)),
                            (r_wgubf[par], rx1g[g][k]), (r_pb[bank],)))
                    P.op("tensor", "pe", 1, *parts)
                sgap = v32(o_sg + 2048 * pidx, TG)
                P.op("scalar", "act", 1,
                     (lambda e, gbank=gbank, sgap=sgap: e.activation(
                         out=sgap, in_=pb[gbank], func=AF.Silu, bias=0.0, scale=S_DEQ),
                      (r_pb[gbank],), (r_sg[pidx],)))
                hsl = h[:, hc * 1024 + gi * TG: hc * 1024 + (gi + 1) * TG]
                P.op("vector", "dve", 1,
                     (lambda e, hsl=hsl, sgap=sgap, ubank=ubank: e.tensor_tensor(
                         out=hsl, in0=pb[ubank], in1=sgap, op=ALU.mult),
                      (r_pb[ubank], r_sg[pidx]), (r_h,)))
        for gi in range(2):
            g = 2 * hh + gi
            pg = g % 2
            ybase = y2off[pg]
            for oc in range(NCH):
                par = wd_slot[0] % 4
                wd_slot[0] += 1
                GQ.dma(lambda e, oc=oc, par=par: e.dma_start(
                           out=v16(o_wdbf + 6144 * par, 3072), in_=wd_d[oc]),
                       (), (r_wdbf[par],))
                wdbf_ap = v16(o_wdbf + 6144 * par, 3072)
                bank = dwn_rr[0] % 4
                dwn_rr[0] += 1
                parts = []
                for hc in range(HCN):
                    parts.append((
                        lambda e, bank=bank, hc=hc, gi=gi, wdbf_ap=wdbf_ap: e.matmul(
                            pb[bank], wdbf_ap[:, 128 * hc:128 * (hc + 1)],
                            h[:, hc * 1024 + gi * TG: hc * 1024 + (gi + 1) * TG],
                            start=(hc == 0), stop=(hc == HCN - 1)),
                        (r_wdbf[par], r_h), (r_pb[bank],)))
                P.op("tensor", "pe", 1, *parts)
                ds_ = dtmp_rr[0] % 4
                dtmp_rr[0] += 1
                dtm = v32(o_dtmp + 2048 * ds_, TG)
                P.op("scalar", "act", 1,
                     (lambda e, bank=bank, dtm=dtm: e.activation(
                         out=dtm, in_=pb[bank], func=AF.Identity, bias=0.0, scale=S_DEQ2),
                      (r_pb[bank],), (r_dtmp[ds_],)))
                step_ln2()
                step_ln2()
                y2sl = v32(ybase + oc * TG * 4, TG)
                x1sl = x1bf[:, oc * TOK + g * TG: oc * TOK + (g + 1) * TG]
                tt(y2sl, dtm, x1sl, ALU.add, [r_dtmp[ds_], rx1g[g][oc]],
                   [r_y2p[pg][oc // 2]])
                y2bfp = v16(o_ybf + 8192 * pg, NCH * TG)
                P.op("scalar", "act", 1,
                     (lambda e, y2sl=y2sl, y2bfp=y2bfp, oc=oc: e.activation(
                         out=y2bfp[:, oc * TG:(oc + 1) * TG], in_=y2sl,
                         func=AF.Identity, bias=0.0, scale=1.0),
                      (r_y2p[pg][oc // 2],), (r_ybfp[pg],)))
                tt(_sq_ap(oc), y2bfp[:, oc * TG:(oc + 1) * TG],
                   y2bfp[:, oc * TG:(oc + 1) * TG], ALU.mult, [r_ybfp[pg]],
                   [r_ysqc[oc]])
            pending_stages.extend(ln2_stages(g))

    flush_ln2()
    for (ssem, sval) in store_toks:
        P.raw_wait("sync", ssem, sval)

    with nc.Block() as block:
        for eng in _ENGS:
            def body(h_, eng=eng):
                for builder, waits, incspec in P.ops[eng]:
                    for s, v in waits:
                        h_.wait_ge(sems[s], v)
                    if builder is not None:
                        ins = builder(h_)
                        if incspec is not None:
                            ins.then_inc(sems[incspec[0]], incspec[1])
            getattr(block, eng)(body)

    return nc


def _pack_lhsT(w_t, n_out_chunks):
    d_in, d_out = w_t.shape
    k = d_in // 128
    a = w_t.reshape(k, 128, n_out_chunks, 128)
    return np.ascontiguousarray(a.transpose(2, 1, 0, 3)).reshape(n_out_chunks, 128, k * 128)


def _host_pack(inputs):
    w_qkv = np.asarray(inputs["L0_attn_W_qkv"], dtype=np.float32)
    b_qkv = np.asarray(inputs["L0_attn_b_qkv"], dtype=np.float32)
    w_out = np.asarray(inputs["L0_attn_W_out"], dtype=np.float32)
    b_out = np.asarray(inputs["L0_attn_b_out"], dtype=np.float32)
    wq_w = np.asarray(inputs["L0_attn_weight_q"], dtype=np.float32)
    wv_w = np.asarray(inputs["L0_attn_weight_v"], dtype=np.float32)
    w_gate = np.asarray(inputs["L0_ffn_w_gate"])
    w_up = np.asarray(inputs["L0_ffn_w_up"])
    w_down = np.asarray(inputs["L0_ffn_w_down"])

    wq = _pack_lhsT(np.ascontiguousarray(w_qkv[0:1024].T), NCH)
    wv = _pack_lhsT(np.ascontiguousarray(w_qkv[2048:3072].T), NCH)
    wo = _pack_lhsT(np.ascontiguousarray(w_out.T), NCH)
    def _to_i8(a):
        assert a.dtype == np.int32 and a.dtype.byteorder in ("=", "<")
        return np.ascontiguousarray(a.view(np.int8)[..., 0::4])

    wg_p = _pack_lhsT(np.ascontiguousarray(w_gate.T), HCN)
    wu_p = _pack_lhsT(np.ascontiguousarray(w_up.T), HCN)
    wgu = _to_i8(np.ascontiguousarray(np.stack([wg_p, wu_p], axis=1)))
    wd = _to_i8(_pack_lhsT(np.ascontiguousarray(w_down.T), NCH))
    wfeat = np.ascontiguousarray(np.concatenate(
        [wq_w.transpose(0, 2, 1).reshape(D, 3), wv_w.transpose(0, 2, 1).reshape(D, 3)],
        axis=1))
    biasp = np.ascontiguousarray(np.stack(
        [b_qkv[0:1024], b_qkv[2048:3072], b_out], axis=1))

    return {"wq": wq, "wv": wv, "wo": wo, "wgu": wgu, "wd": wd,
            "wfeat": wfeat, "biasp": biasp}


def kernel(**inputs):
    x = np.asarray(inputs["L0_attn_query"], dtype=np.float32)
    common = _host_pack(inputs)
    _BREV = [0, 4, 2, 6, 1, 5, 3, 7]
    xt = x.reshape(16384 // 8, 8, D)[:, _BREV, :].reshape(16384, D)
    shards = [np.ascontiguousarray(
        xt[i * TOK:(i + 1) * TOK].T.reshape(NCH, 128, TOK)
        .transpose(1, 0, 2).reshape(128, NCH * TOK)) for i in range(NCORES)]

    nc = _build_nc()
    in_maps = [dict(common, x=shards[i]) for i in range(NCORES)]
    res = run_bass_kernel_spmd(nc, in_maps, core_ids=list(range(NCORES)))

    out = np.concatenate([np.asarray(res.results[i]["y"]).T for i in range(NCORES)], axis=0)
    out = out.reshape(16384 // 8, 8, D)[:, _BREV, :]
    return np.ascontiguousarray(out.reshape(2, 8192, D).astype(np.float32))


# version 37
# speedup vs baseline: 1.0078x; 1.0078x over previous
import numpy as np
import concourse.bass as bass
import concourse.mybir as mybir
from concourse.bass_utils import run_bass_kernel_spmd

F32 = mybir.dt.float32
BF16 = mybir.dt.bfloat16
I32 = mybir.dt.int32
I8 = mybir.dt.int8
U8 = mybir.dt.uint8
AF = mybir.ActivationFunctionType
ALU = mybir.AluOpType

NCORES = 8
D = 1024
TOK = 2048
NCH = 8
NG = 4
TG = 512
HCN = 24
S_DEQ = 1.0 / 127.0
S_DEQ2 = S_DEQ * S_DEQ
EPS = 1e-5

_ENGS = ("sync", "scalar", "tensor", "vector", "gpsimd")


class _Res:
    __slots__ = ("w", "readers")

    def __init__(self):
        self.w = None
        self.readers = []

    def inherit(self, *olds):
        for o in olds:
            if o.w is not None:
                self.readers.append(o.w)
            self.readers.extend(o.readers)
        return self


class _Plan:
    def __init__(self):
        self.semval = {}
        self.ops = {e: [] for e in _ENGS}
        self.waited = {}

    def _tok_waits(self, eng, toks):
        need = {}
        for t in toks:
            if t is None:
                continue
            s, v = t
            if need.get(s, 0) < v:
                need[s] = v
        out = []
        for s, v in need.items():
            if self.waited.get((eng, s), 0) < v:
                out.append((s, v))
                self.waited[(eng, s)] = v
        return out

    def op(self, eng, sem, inc, *parts):
        self.semval[sem] = self.semval.get(sem, 0) + inc
        tok = (sem, self.semval[sem])
        n = len(parts)
        for i, part in enumerate(parts):
            builder, reads, writes = part[0], part[1], part[2]
            toks = []
            for r in reads:
                toks.append(r.w)
            for w in writes:
                toks.append(w.w)
                toks.extend(w.readers)
            waits = self._tok_waits(eng, toks)
            self.ops[eng].append((builder, waits, (sem, inc) if i == n - 1 else None))
        for part in parts:
            builder, reads, writes = part[0], part[1], part[2]
            war = part[3] if len(part) > 3 else ()
            for r in reads:
                r.readers.append(tok)
            for r in war:
                r.readers.append(tok)
            for w in writes:
                w.w = tok
                w.readers = []
        return tok

    def raw_wait(self, eng, sem, val):
        if self.waited.get((eng, sem), 0) < val:
            self.ops[eng].append((None, [(sem, val)], None))
            self.waited[(eng, sem)] = val


_DMA_POOL = 12
_DMA_INFLIGHT = 6


class _DmaRR:

    def __init__(self, plan, eng, prefix):
        self.plan = plan
        self.eng = eng
        self.prefix = prefix
        self.idx = 0
        self.hist = []

    def dma(self, builder, reads, writes):
        k = self.idx
        self.idx += 1
        sem = f"{self.prefix}{k % _DMA_POOL}"
        occ = k // _DMA_POOL + 1
        if occ > 1:
            self.plan.raw_wait(self.eng, sem, 16 * (occ - 1))
        self.hist.append((sem, occ))
        return self.plan.op(self.eng, sem, 16, (builder, tuple(reads), tuple(writes)))


def _build_nc():
    nc = bass.Bass("TRN2", target_bir_lowering=False, debug=False)

    x_d = nc.dram_tensor("x", [128, NCH * TOK], F32, kind="ExternalInput").ap()
    wq_d = nc.dram_tensor("wq", [NCH, 128, 8 * 128], F32, kind="ExternalInput").ap()
    wv_d = nc.dram_tensor("wv", [NCH, 128, 8 * 128], F32, kind="ExternalInput").ap()
    wo_d = nc.dram_tensor("wo", [NCH, 128, 8 * 128], F32, kind="ExternalInput").ap()
    wgu_d = nc.dram_tensor("wgu", [HCN, 2, 128, 8 * 128], I8, kind="ExternalInput").ap()
    wd_d = nc.dram_tensor("wd", [NCH, 128, HCN * 128], I8, kind="ExternalInput").ap()
    wfeat_d = nc.dram_tensor("wfeat", [D, 6], F32, kind="ExternalInput").ap()
    biasp_d = nc.dram_tensor("biasp", [D, 3], F32, kind="ExternalInput").ap()
    y_d = nc.dram_tensor("y", [D, TOK], F32, kind="ExternalOutput").ap()
    wos_d = nc.dram_tensor("wos", [NCH, 128, 8 * 128], BF16, kind="Internal").ap()

    AB = 210944
    _resv = nc.alloc_sbuf_tensor("resv", [128, AB], U8)
    BASE = nc.lookup_mloc(_resv).addr
    _reg32 = []
    _reg16 = []

    def _sb(name, off, nelems, dtype, reg, esz):
        h = nc.alloc_sbuf_tensor_at(name, [128, nelems], dtype, offset=BASE + off)
        reg.append((off, off + nelems * esz, h.ap()))
        return h

    def _route(reg, off, nbytes):
        best = None
        for a, b, ap_ in reg:
            if a <= off and off + nbytes <= b:
                if best is None or (b - a) < (best[1] - best[0]):
                    best = (a, b, ap_)
        assert best is not None, (off, nbytes)
        return best

    def v32(off, n):
        a, b, ap_ = _route(_reg32, off, 4 * n)
        return ap_[:, (off - a) // 4: (off - a) // 4 + n]

    def v16(off, n):
        a, b, ap_ = _route(_reg16, off, 2 * n)
        return ap_[:, (off - a) // 2: (off - a) // 2 + n]

    o_consts = 0
    o_ones = 64
    o_bias = 320
    o_wfeat = 448
    o_alpha = 640
    o_msb = 768
    o_var = 2816
    o_r = 4864
    o_mr = 8960
    o_rbf = 13056
    o_mrbf = 15104
    o_ntmp = 17152
    o_sg = 21248
    o_X = 25600
    o_mix = 91136
    S0 = 123904
    o_xbf = S0
    o_qt = S0 + 32768
    o_vt = S0 + 36864
    o_s1 = S0 + 40960
    o_d1 = S0 + 45056
    o_s2 = S0 + 49152
    o_d2 = S0 + 51200
    o_s3 = S0 + 53248
    o_d3 = S0 + 54272
    o_wstbf = S0 + 55296
    o_wox = S0 + 32768
    o_ybf = S0 + 45056
    o_ysqA = S0 + 61440
    o_ysqB = S0 + 81920
    o_x1bf = S0
    o_etmp = S0 + 65536
    o_h = o_X
    o_y2 = o_X + 49152
    o_wgux = S0 + 32768
    o_wdbf = o_mix
    o_y2b = S0 + 65536
    o_y2bf = o_ybf

    for nm, off, ne in (
        ("t_cons", o_consts, 16), ("t_bias", o_bias, 24), ("t_wfeat", o_wfeat, 48),
        ("t_alpha", o_alpha, 32), ("t_msb", o_msb, 512), ("t_var", o_var, 512),
        ("t_rr", o_r, 1024), ("t_mr", o_mr, 1024), ("t_ntmp32", o_ntmp, 1024),
        ("t_sg", o_sg, 1024), ("t_X", o_X, 16384),
        ("t_fsc", o_etmp, 4096),
        ("t_dtmp", o_mix + 24576, 2048),
    ):
        _sb(nm, off, ne, F32, _reg32, 4)
    for nm, off, ne in (
        ("t_ones", o_ones, 128), ("t_rbf", o_rbf, 1024), ("t_mrbf", o_mrbf, 1024),
        ("t_ntmpbf", o_ntmp, 2048), ("t_mix", o_mix, 16384), ("t_h", o_X, 24576),
        ("t_sc16", S0, (209920 - S0) // 2),
    ):
        _sb(nm, off, ne, BF16, _reg16, 2)

    pb = [nc.alloc_psum_tensor(f"pb{i}", [128, 512], F32).ap() for i in range(8)]

    sems = {}
    for s in (["out", "pe", "act", "dve", "gp"]
              + [f"dq{i}" for i in range(_DMA_POOL)]
              + [f"gq{i}" for i in range(_DMA_POOL)]
              + [f"aq{i}" for i in range(_DMA_POOL)]):
        cm = nc.semaphore(s)
        sems[s] = cm.__enter__()

    P = _Plan()
    DQ = _DmaRR(P, "sync", "dq")
    GQ = _DmaRR(P, "gpsimd", "gq")
    AQ = _DmaRR(P, "scalar", "aq")

    rXg = [_Res() for _ in range(NG)]
    rXc = [_Res() for _ in range(NCH)]
    rxbf = [_Res() for _ in range(NCH)]
    r_mixc = [_Res() for _ in range(NCH)]
    r_mix7a = _Res()
    r_qt = _Res()
    r_vt = _Res()
    r_cqp = [_Res(), _Res()]
    r_cvp = [_Res(), _Res()]
    r_wst = _Res()
    r_wos = [_Res() for _ in range(NCH)]
    r_wstbf = [_Res() for _ in range(4)]
    r_consts = _Res()
    r_ones = _Res()
    r_bias = _Res()
    r_wfeat = _Res()
    r_alpha = _Res()
    r_pb = [_Res() for _ in range(8)]
    r_msb = [_Res(), _Res()]
    r_var = _Res()
    r_r = [_Res(), _Res()]
    r_mr = [_Res(), _Res()]
    r_ntmp = [_Res(), _Res()]
    r_rbf = [_Res(), _Res()]
    r_mrbf = [_Res(), _Res()]
    r_sg = [_Res(), _Res()]

    cons = v32(o_consts, 16)
    ones_ap = v16(o_ones, 128)

    def ms(ap_, val, res):
        P.op("gpsimd", "gp", 1, (lambda e, a=ap_, v=val: e.memset(a, v), (), (res,)))

    def tt(out, i0, i1, op, reads, writes):
        P.op("vector", "dve", 1,
             (lambda e, out=out, i0=i0, i1=i1, op=op: e.tensor_tensor(out=out, in0=i0, in1=i1, op=op),
              tuple(reads), tuple(writes)))

    def tt_g(out, i0, i1, op, reads, writes):
        P.op("gpsimd", "gp", 1,
             (lambda e, out=out, i0=i0, i1=i1, op=op: e.tensor_tensor(out=out, in0=i0, in1=i1, op=op),
              tuple(reads), tuple(writes)))

    def stt_g(out, i0, sc, i1, op0, op1, reads, writes):
        assert op0 == ALU.mult and op1 == ALU.mult
        P.op("gpsimd", "gp", 1,
             (lambda e, out=out, i0=i0, i1=i1: e.tensor_tensor(out=out, in0=i0, in1=i1, op=ALU.mult),
              tuple(reads), tuple(writes)))
        P.op("vector", "dve", 1,
             (lambda e, out=out, sc=sc: e.tensor_single_scalar(out=out, in_=out, scalar=sc, op=ALU.mult),
              tuple(writes), tuple(writes)))

    def stt(out, i0, sc, i1, op0, op1, reads, writes):
        P.op("vector", "dve", 1,
             (lambda e, out=out, i0=i0, sc=sc, i1=i1, op0=op0, op1=op1: e.scalar_tensor_tensor(
                 out=out, in0=i0, scalar=sc, in1=i1, op0=op0, op1=op1),
              tuple(reads), tuple(writes)))

    ms(cons[:, 0:1], EPS, r_consts)
    ms(cons[:, 5:6], 0.5, r_consts)
    ms(ones_ap, 1.0 / 1024.0, r_ones)
    P.op("scalar", "act", 1,
         (lambda e: e.activation(out=cons[:, 5:6], in_=cons[:, 5:6], func=AF.Sqrt,
                                 bias=0.0, scale=1.0), (r_consts,), (r_consts,)))
    tt(cons[:, 2:3], cons[:, 5:6], cons[:, 5:6], ALU.mult, [r_consts], [r_consts])
    tt(cons[:, 2:3], cons[:, 2:3], cons[:, 5:6], ALU.mult, [r_consts], [r_consts])
    tt(cons[:, 4:5], cons[:, 2:3], cons[:, 2:3], ALU.mult, [r_consts], [r_consts])
    tt(cons[:, 3:4], cons[:, 4:5], cons[:, 2:3], ALU.mult, [r_consts], [r_consts])

    X3 = v32(o_X, NCH * TOK).rearrange("p (c t) -> p c t", t=TOK)
    Xflat = v32(o_X, NCH * TOK)
    DQ.dma(lambda e: e.dma_start(out=v32(o_wfeat, 48).rearrange("p (c k) -> p c k", k=6),
                                 in_=wfeat_d.rearrange("(c p) k -> p c k", p=128)),
           (), (r_wfeat,))
    DQ.dma(lambda e: e.dma_start(out=v32(o_bias, 24).rearrange("p (c k) -> p c k", k=3),
                                 in_=biasp_d.rearrange("(c p) k -> p c k", p=128)),
           (), (r_bias,))
    rXdq, rXgq = _Res(), _Res()
    xbf = v16(o_xbf, NCH * TOK)

    def _load_x_and_cast():
        for c in range(NCH):
            Q = DQ if c < 4 else GQ
            wr = (rXc[c],) + ((rXdq,) if c == 3 else (rXgq,) if c == 7 else ())
            Q.dma(lambda e, c=c: e.dma_start(out=Xflat[:, c * TOK:(c + 1) * TOK],
                                             in_=x_d[:, c * TOK:(c + 1) * TOK]),
                  (), wr)
        for c in range(NCH):
            if c % 2 == 0:
                P.op("scalar", "act", 1,
                     (lambda e, c=c: e.activation(out=xbf[:, c * TOK:(c + 1) * TOK],
                                                  in_=X3[:, c, :], func=AF.Identity,
                                                  bias=0.0, scale=1.0),
                      (rXc[c],), (rxbf[c],), tuple(rXg)))
            else:
                P.op("vector", "dve", 1,
                     (lambda e, c=c: e.tensor_copy(xbf[:, c * TOK:(c + 1) * TOK],
                                                   X3[:, c, :]),
                      (rXc[c],), (rxbf[c],), tuple(rXg)))
        P.op("vector", "dve", 1,
             (lambda e: e.tensor_copy(cons[:, 15:16], cons[:, 5:6]),
              (rXdq, rXgq, r_consts), tuple(rXg)))

    wf3 = v32(o_wfeat, 48).rearrange("p (c k) -> p c k", k=6)
    al3 = v32(o_alpha, 32).rearrange("p (c k) -> p c k", k=4)
    for j, scol in ((0, 3), (1, 3), (2, 4)):
        P.op("vector", "dve", 1,
             (lambda e, j=j, scol=scol: e.scalar_tensor_tensor(
                 out=al3[:, :, j], in0=wf3[:, :, j], scalar=cons[:, scol:scol + 1],
                 in1=wf3[:, :, 3 + j], op0=ALU.mult, op1=ALU.mult),
              (r_wfeat, r_consts), (r_alpha,)))

    qt = v16(o_qt, TOK)
    vt = v16(o_vt, TOK)
    mix = v16(o_mix, NCH * TOK)
    bias3 = v32(o_bias, 24).rearrange("p (c k) -> p c k", k=3)

    wst_slot = [0]

    def stream_slab(src_ap):
        slot = wst_slot[0] % 4
        wst_slot[0] += 1
        GQ.dma(lambda e, src=src_ap, p=slot: e.dma_start(
                   out=v16(o_wstbf + 2048 * p, 1024), in_=src),
               (), (r_wstbf[slot],))
        return v16(o_wstbf + 2048 * slot, 1024), r_wstbf[slot]

    bank_rr = [0]

    def proj_chunk(c, wbf, wres, dst_ap, dst_res, bcol, rhs_ap, rhs_res):
        for g in range(NG):
            bank = bank_rr[0] % 4
            bank_rr[0] += 1
            parts = []
            for k in range(8):
                parts.append((
                    lambda e, bank=bank, k=k, g=g, wbf=wbf: e.matmul(
                        pb[bank], wbf[:, 128 * k:128 * (k + 1)],
                        rhs_ap[:, k * TOK + g * TG: k * TOK + (g + 1) * TG],
                        start=(k == 0), stop=(k == 7)),
                    (wres, rhs_res[k]), (r_pb[bank],)))
            P.op("tensor", "pe", 1, *parts)
            P.op("scalar", "act", 1,
                 (lambda e, bank=bank, g=g, c=c, dst_ap=dst_ap, bcol=bcol: e.activation(
                     out=dst_ap[:, g * TG:(g + 1) * TG], in_=pb[bank],
                     func=AF.Identity, bias=bias3[:, c, bcol:bcol + 1], scale=1.0),
                  (r_pb[bank], r_bias), (dst_res,)))

    def _halves(ap_, blk):
        v = ap_.rearrange("p (b i) -> p b i", i=blk)
        return v[:, :, 0:blk // 2], v[:, :, blk // 2:blk]

    def mixer(c):
        qv = v16(o_qt, 2 * TOK)
        s1, d1 = v16(o_s1, 2048), v16(o_d1, 2048)
        s2, d2 = v16(o_s2, 1024), v16(o_d2, 1024)
        s3, d3 = v16(o_s3, 512), v16(o_d3, 512)
        d1q, d1v = d1[:, 0:1024], d1[:, 1024:2048]
        d2q, d2v = d2[:, 0:512], d2[:, 512:1024]
        s3q, s3v = s3[:, 0:256], s3[:, 256:512]
        d3q, d3v = d3[:, 0:256], d3[:, 256:512]
        rc = r_cqp[0]
        e, o = _halves(qv, 8)
        tt(s1.rearrange("p (b i) -> p b i", i=4), e, o, ALU.add, [r_qt, r_vt], [rc])
        tt(d1.rearrange("p (b i) -> p b i", i=4), e, o, ALU.subtract, [r_qt, r_vt], [rc])
        stt(d1q, d1q, cons[:, 2:3], d1v, ALU.mult, ALU.mult, [rc, r_consts], [rc])
        e, o = _halves(s1, 4)
        tt(s2.rearrange("p (b i) -> p b i", i=2), e, o, ALU.add, [rc], [rc])
        tt(d2.rearrange("p (b i) -> p b i", i=2), e, o, ALU.subtract, [rc], [rc])
        stt(d2q, d2q, al3[:, c, 2:3], d2v, ALU.mult, ALU.mult, [rc, r_alpha], [rc])
        e, o = _halves(s2, 2)
        tt(s3.rearrange("p (b i) -> p b i", i=1), e, o, ALU.add, [rc], [rc])
        tt(d3.rearrange("p (b i) -> p b i", i=1), e, o, ALU.subtract, [rc], [rc])
        stt(s3q, s3q, al3[:, c, 0:1], s3v, ALU.mult, ALU.mult, [rc, r_alpha], [rc])
        stt(d3q, d3q, al3[:, c, 1:2], d3v, ALU.mult, ALU.mult, [rc, r_alpha], [rc])
        a2 = s2[:, 512:1024]
        a2e, a2o = _halves(a2, 2)
        tt(a2e, s3q.rearrange("p (b i) -> p b i", i=1), d3q.rearrange("p (b i) -> p b i", i=1), ALU.add, [rc], [rc])
        tt(a2o, s3q.rearrange("p (b i) -> p b i", i=1), d3q.rearrange("p (b i) -> p b i", i=1), ALU.subtract, [rc], [rc])
        a1 = s1[:, 0:1024]
        a1e, a1o = _halves(a1, 4)
        tt(a1e, a2.rearrange("p (b i) -> p b i", i=2), d2q.rearrange("p (b i) -> p b i", i=2), ALU.add, [rc], [rc])
        tt(a1o, a2.rearrange("p (b i) -> p b i", i=2), d2q.rearrange("p (b i) -> p b i", i=2), ALU.subtract, [rc], [rc])
        if c == 7:
            for hh_, res_ in ((0, r_mix7a), (1, r_mixc[7])):
                msl = mix[:, c * TOK + 1024 * hh_: c * TOK + 1024 * (hh_ + 1)]
                me, mo = _halves(msl, 8)
                a1h = a1[:, 512 * hh_:512 * (hh_ + 1)].rearrange("p (b i) -> p b i", i=4)
                d1h = d1q[:, 512 * hh_:512 * (hh_ + 1)].rearrange("p (b i) -> p b i", i=4)
                tt(me, a1h, d1h, ALU.add, [rc], [res_])
                tt(mo, a1h, d1h, ALU.subtract, [rc], [res_])
        else:
            mslice = mix[:, c * TOK:(c + 1) * TOK]
            me, mo = _halves(mslice, 8)
            tt(me, a1.rearrange("p (b i) -> p b i", i=4), d1q.rearrange("p (b i) -> p b i", i=4), ALU.add, [rc], [r_mixc[c]])
            tt(mo, a1.rearrange("p (b i) -> p b i", i=4), d1q.rearrange("p (b i) -> p b i", i=4), ALU.subtract, [rc], [r_mixc[c]])

    def proj_chunk0(w0q, w0v):
        streams = ((w0q[0], w0q[1], qt, r_qt, 0), (w0v[0], w0v[1], vt, r_vt, 1))
        for k in range(7):
            for si in range(2):
                wap, wres = streams[si][0], streams[si][1]
                for g in range(NG):
                    bank = 4 * si + g
                    P.op("tensor", "pe", 1,
                         (lambda e, bank=bank, k=k, g=g, wap=wap: e.matmul(
                             pb[bank], wap[:, 128 * k:128 * (k + 1)],
                             xbf[:, k * TOK + g * TG: k * TOK + (g + 1) * TG],
                             start=(k == 0), stop=False),
                          (wres, rxbf[k]), (r_pb[bank],)))
        for si in range(2):
            wap, wres, dst, dres, bcol = streams[si]
            for g in range(NG):
                bank = 4 * si + g
                P.op("tensor", "pe", 1,
                     (lambda e, bank=bank, wap=wap, g=g: e.matmul(
                         pb[bank], wap[:, 896:1024],
                         xbf[:, 7 * TOK + g * TG: 7 * TOK + (g + 1) * TG],
                         start=False, stop=True),
                      (wres, rxbf[7]), (r_pb[bank],)))
                P.op("scalar", "act", 1,
                     (lambda e, bank=bank, g=g, dst=dst, bcol=bcol: e.activation(
                         out=dst[:, g * TG:(g + 1) * TG], in_=pb[bank],
                         func=AF.Identity, bias=bias3[:, 0, bcol:bcol + 1],
                         scale=1.0),
                      (r_pb[bank], r_bias), (dres,)))
        bank_rr[0] = 0

    w0q = stream_slab(wq_d[0])
    w0v = stream_slab(wv_d[0])
    _load_x_and_cast()

    for c in range(NCH):
        if c == 0:
            proj_chunk0(w0q, w0v)
        else:
            wap, wres = stream_slab(wq_d[c])
            proj_chunk(c, wap, wres, qt, r_qt, 0, xbf, rxbf)
            wap, wres = stream_slab(wv_d[c])
            proj_chunk(c, wap, wres, vt, r_vt, 1, xbf, rxbf)
        woap, wores = stream_slab(wo_d[c])
        DQ.dma(lambda e, c=c, woap=woap: e.dma_start(out=wos_d[c], in_=woap),
               (wores,), (r_wos[c],))
        mixer(c)

    x1bf = v16(o_x1bf, NCH * TOK)
    rx1g = [[_Res().inherit(*rxbf) for _ in range(NCH)] for _ in range(NG)]


    wo_slot = [0]
    r_woslot = [_Res().inherit(r_qt), _Res().inherit(r_qt),
                _Res().inherit(r_vt), _Res().inherit(r_vt),
                _Res().inherit(*r_cqp), _Res().inherit(*r_cqp)]
    r_ybfp = [_Res().inherit(*r_cqp, *r_cvp), _Res().inherit(*r_cqp, *r_cvp)]
    r_ysqc = [_Res().inherit(r_wstbf[3]), _Res().inherit(r_wstbf[3])] \
        + [_Res() for _ in range(6)]
    E_OFF = [S0 + 77824, S0 + 79872, o_wstbf, o_wstbf + 2048, o_wstbf + 4096]
    r_wosl_e = [_Res(), _Res(),
                _Res().inherit(r_wstbf[0]), _Res().inherit(r_wstbf[1]),
                _Res().inherit(r_wstbf[2])]

    def _sq_ap(c):
        if c < 4:
            return v16(o_ysqA + 1024 * c, TG)
        return v16(o_ysqB + 1024 * (c - 4), TG)

    def _wo_slot_ap(p):
        return v16(o_wox + 2048 * p, 1024)

    def wout_slab(g, oc):
        idx = wo_slot[0]
        wo_slot[0] += 1
        if idx < 5:
            ap_, res = v16(E_OFF[idx], 1024), r_wosl_e[idx]
        else:
            slot = (idx - 5) % 8
            if slot < 6:
                ap_, res = _wo_slot_ap(slot), r_woslot[slot]
            else:
                ap_, res = v16(E_OFF[slot - 6], 1024), r_wosl_e[slot - 6]
        DQ.dma(lambda e, oc=oc, ap_=ap_: e.dma_start(out=ap_, in_=wos_d[oc]),
               (r_wos[oc],), (res,))
        return ap_, res

    o_etmp = o_y2b
    r_etmp = [_Res().inherit(*r_cvp) for _ in range(6)]
    etmp_rr = [0]

    WBANKS = [0, 1, 2, 3, 6, 7]
    wout_rr = [0]

    def _wout_mm(g, oc, bank, wbf, wres, ks, start, stop, k7res=None):
        parts = []
        for k in ks:
            mres = k7res if (k == 7 and k7res is not None) else r_mixc[k]
            parts.append((
                lambda e, bank=bank, k=k, g=g, wbf=wbf, s=start, st=stop, ks=ks: e.matmul(
                    pb[bank], wbf[:, 128 * k:128 * (k + 1)],
                    mix[:, k * TOK + g * TG: k * TOK + (g + 1) * TG],
                    start=(s and k == ks[0]), stop=(st and k == ks[-1])),
                (wres, mres), (r_pb[bank],)))
        P.op("tensor", "pe", 1, *parts)

    def _wout_tail(g, oc, bank, ybfp):
        es_ = etmp_rr[0] % 6
        etmp_rr[0] += 1
        tmp = v32(o_etmp + 2048 * es_, TG)
        P.op("scalar", "act", 1,
             (lambda e, bank=bank, tmp=tmp, oc=oc: e.activation(
                 out=tmp, in_=pb[bank], func=AF.Identity,
                 bias=bias3[:, oc, 2:3], scale=1.0),
              (r_pb[bank], r_bias), (r_etmp[es_],)))
        xsl = v32(o_X + (oc * TOK + g * TG) * 4, TG)
        ybf_w = [r_ybfp[g % 2]] + (r_wosl_e[2:] if g == 1 else [])
        tt(ybfp[:, oc * TG:(oc + 1) * TG], tmp, xsl, ALU.add,
           [r_etmp[es_], rXg[g]], ybf_w)
        pending_sqs.append((oc, ybfp, g % 2))

    PBANKS = [0, 1, 2, 3, 6, 7, 4, 5]

    def wout_group(g, mid_cb=None):
        ybfp = v16(o_ybf + 8192 * (g % 2), NCH * TG)
        if g == 0:
            info = {}
            for oc in range(NCH):
                wbf, wres = wout_slab(g, oc)
                bank = PBANKS[oc]
                info[oc] = (bank, wbf, wres)
                _wout_mm(g, oc, bank, wbf, wres, list(range(7)), True, False)
            for oc in range(NCH):
                bank, wbf, wres = info[oc]
                _wout_mm(g, oc, bank, wbf, wres, [7], False, True,
                         k7res=r_mix7a)
                _wout_tail(g, oc, bank, ybfp)
            wout_rr[0] = 0
        else:
            for oc in range(NCH):
                wbf, wres = wout_slab(g, oc)
                bank = WBANKS[wout_rr[0] % 6]
                wout_rr[0] += 1
                _wout_mm(g, oc, bank, wbf, wres, list(range(8)), True, True)
                if oc < 4:
                    step_wout_norm()
                _wout_tail(g, oc, bank, ybfp)

    ln_par = [0]

    def ln_group(src3, src_res, bfap, bfres, outs, bf_norm=False,
                 defer_norm=False, mb=(4, 5), pre_bf=False):
        if not pre_bf:
            P.op("scalar", "act", 1,
                 (lambda e, src3=src3: e.activation(
                     out=bfap.rearrange("p (c t) -> p c t", t=TG), in_=src3,
                     func=AF.Identity, bias=0.0, scale=1.0),
                  (src_res,), (bfres,)))
            for c in range(NCH):
                tt(_sq_ap(c), bfap[:, c * TG:(c + 1) * TG],
                   bfap[:, c * TG:(c + 1) * TG],
                   ALU.mult, [bfres], [r_ysqc[c]])
        bm, bs = mb
        parts = []
        for c in range(NCH):
            parts.append((lambda e, c=c, bm=bm: e.matmul(pb[bm], ones_ap, bfap[:, c * TG:(c + 1) * TG],
                                                         start=(c == 0), stop=(c == NCH - 1)),
                          (r_ones, bfres), (r_pb[bm],)))
        P.op("tensor", "pe", 1, *parts)
        parts = []
        for c in range(NCH):
            parts.append((lambda e, c=c, bs=bs: e.matmul(
                              pb[bs], ones_ap, _sq_ap(c),
                              start=(c == 0), stop=(c == NCH - 1)),
                          (r_ones, r_ysqc[c]), (r_pb[bs],)))
        P.op("tensor", "pe", 1, *parts)
        par = ln_par[0] % 2
        ln_par[0] += 1
        msb, var = v32(o_msb if par == 0 else o_mr, TG), v32(o_var, TG)
        rr = v32(o_r + 2048 * par, TG)
        P.op("scalar", "act", 1,
             (lambda e, bm=bm, msb=msb: e.activation(out=msb, in_=pb[bm], func=AF.Identity, bias=0.0, scale=1.0),
              (r_pb[bm],), (r_msb[par],)))
        tt(var, msb, msb, ALU.mult, [r_msb[par]], [r_var])
        tt(var, pb[bs], var, ALU.subtract, [r_pb[bs], r_var], [r_var])
        P.op("scalar", "act", 1,
             (lambda e, rr=rr: e.activation(out=rr, in_=var, func=AF.Ln, bias=cons[:, 0:1], scale=1.0),
              (r_var, r_consts), (r_r[par],)))
        P.op("scalar", "act", 1,
             (lambda e, rr=rr: e.activation(out=rr, in_=rr, func=AF.Exp, bias=0.0, scale=-0.5),
              (r_r[par],), (r_r[par],)))
        def _norm(lo=0, hi=NCH, bf_norm=bf_norm, outs=outs, src_res=src_res,
                  bfap=bfap, bfres=bfres, rr=rr, msb=msb, par=par):
            _emit_norm(bf_norm, outs, src_res, bfap, bfres, rr, msb, par, lo, hi)
        if defer_norm:
            return _norm
        _norm()
        return None

    def _emit_norm(bf_norm, outs, src_res, bfap, bfres, rr, msb, par, lo=0, hi=NCH):
        if bf_norm:
            rrb = v16(o_rbf + 1024 * par, TG)
            msbb = v16(o_mrbf + 1024 * par, TG)
            if lo == 0:
                P.op("vector", "dve", 1,
                     (lambda e, rrb=rrb, rr=rr: e.tensor_copy(rrb, rr), (r_r[par],), (r_rbf[par],)))
                P.op("vector", "dve", 1,
                     (lambda e, msbb=msbb, msb=msb: e.tensor_copy(msbb, msb), (r_msb[par],), (r_mrbf[par],)))
            for i in range(lo, hi):
                out_ap, c, out_res = outs[i]
                bfin = bfap[:, c * TG:(c + 1) * TG]
                ntmp = v16(o_ntmp + 2048 * (i % 2), TG)
                tt(ntmp, bfin, msbb, ALU.subtract, [bfres, r_mrbf[par]], [r_ntmp[i % 2]])
                tt(out_ap, ntmp, rrb, ALU.mult, [r_ntmp[i % 2], r_rbf[par]], [out_res])
        else:
            for i in range(lo, hi):
                out_ap, in_ap, out_res = outs[i]
                ntmp = v32(o_ntmp + 2048 * (i % 2), TG)
                tt(ntmp, in_ap, msb, ALU.subtract, [out_res, r_msb[par]], [r_ntmp[i % 2]])
                tt(out_ap, ntmp, rr, ALU.mult, [r_ntmp[i % 2], r_r[par]], [out_res])

    def ln1_group(g):
        outs = []
        for oc in range(NCH):
            out_ap = x1bf[:, oc * TOK + g * TG: oc * TOK + (g + 1) * TG]
            outs.append((out_ap, oc, rx1g[g][oc]))
        ybfp = v16(o_ybf + 8192 * (g % 2), NCH * TG)
        return ln_group(None, rXg[g], ybfp, r_ybfp[g % 2], outs, bf_norm=True,
                        defer_norm=True, pre_bf=True)

    pending_norm = []
    wout_norm_stages = []
    pending_sqs = []

    def step_wout_norm():
        if wout_norm_stages:
            wout_norm_stages.pop(0)()

    def flush_norm():
        while pending_norm:
            pending_norm.pop(0)()

    prev_g = [None]

    def _mid_stats():
        if prev_g[0] is not None:
            pending_norm.append(ln1_group(prev_g[0]))
            prev_g[0] = None

    def _flush_sqs():
        while pending_sqs:
            oc, ybfp_, gpar = pending_sqs.pop(0)
            tt(_sq_ap(oc), ybfp_[:, oc * TG:(oc + 1) * TG],
               ybfp_[:, oc * TG:(oc + 1) * TG], ALU.mult, [r_ybfp[gpar]],
               [r_ysqc[oc]])

    for g in range(NG):
        wout_group(g)
        if prev_g[0] is not None:
            fn = ln1_group(prev_g[0])
            if g < 3:
                for lo in range(0, NCH, 2):
                    wout_norm_stages.append(lambda fn=fn, lo=lo: fn(lo, lo + 2))
            else:
                pending_norm.append(fn)
        _flush_sqs()
        prev_g[0] = g
    while wout_norm_stages:
        wout_norm_stages.pop(0)()


    h = v16(o_h, HCN * 1024)
    r_h = _Res().inherit(*rXg)
    y2off = [o_y2, o_y2b]
    r_y2p = [[_Res().inherit(*rXg) for _ in range(4)],
             [_Res().inherit(*r_cvp, *r_cqp, *r_wstbf, *r_etmp, *r_wosl_e[:2])
              for _ in range(4)]]
    r_wgubf = [_Res().inherit(*r_woslot) for _ in range(3)]

    def _wgu_slot_ap(p):
        return v16(o_wgux + 4096 * p, 2048)
    wgu_slot = [0]
    r_wdbf = [_Res().inherit(*r_mixc, r_mix7a) for _ in range(4)]
    y2bf = v16(o_y2bf, NCH * TG)

    store_toks = []
    wd_slot = [0]
    dwn_rr = [0]
    o_dtmp = o_mix + 24576
    r_dtmp = [_Res().inherit(*r_mixc, r_mix7a) for _ in range(4)]
    dtmp_rr = [0]

    def ln2_stages(g):
        pg = g % 2
        ybase = y2off[pg]
        outs = []
        for oc in range(NCH):
            ap_ = v32(ybase + oc * TG * 4, TG)
            outs.append((ap_, ap_, r_y2p[pg][oc // 2]))
        y2bfp = v16(o_ybf + 8192 * pg, NCH * TG)
        box = [None]

        def s_stats():
            box[0] = ln_group(None, None, y2bfp, r_ybfp[pg], outs,
                              pre_bf=True, defer_norm=True)

        def mk_norm(lo):
            return lambda: box[0](lo, lo + 2)

        def mk_store(lo, w):
            def s_store():
                tok = DQ.dma(lambda e, g=g, ybase=ybase, lo=lo, w=w: e.dma_start(
                         out=y_d.rearrange("(c p) t -> p c t", p=128)[
                             :, lo:lo + w, g * TG:(g + 1) * TG],
                         in_=v32(ybase + lo * TG * 4, w * TG).rearrange(
                             "p (c t) -> p c t", t=TG)),
                      (r_y2p[pg][lo // 2],), ())
                store_toks.append(tok)
            return s_store

        st = [s_stats]
        for lo in range(0, NCH, 2):
            st.append(mk_norm(lo))
            if g == 3:
                st.append(mk_store(lo, 1))
                st.append(mk_store(lo + 1, 1))
            else:
                st.append(mk_store(lo, 2))
        return st

    pending_stages = []

    def flush_ln2():
        while pending_stages:
            pending_stages.pop(0)()

    def step_ln2():
        if pending_stages:
            pending_stages.pop(0)()

    for hh in range(2):
        for hc in range(HCN):
            par = wgu_slot[0] % 3
            wgu_slot[0] += 1
            GQ.dma(lambda e, hc=hc, par=par: e.dma_start(
                       out=_wgu_slot_ap(par).rearrange("p (w f) -> p w f", f=1024),
                       in_=wgu_d[hc].rearrange("w p f -> p w f")),
                   (), (r_wgubf[par],))
            wgubf_ap = _wgu_slot_ap(par)
            if hc == 1:
                _mid_stats()
                while pending_norm:
                    fn = pending_norm.pop(0)
                    for lo in range(0, NCH, 2):
                        pending_stages.append(lambda fn=fn, lo=lo: fn(lo, lo + 2))
            if hc >= 1:
                step_ln2()
            for gi in range(2):
                g = 2 * hh + gi
                pidx = (2 * hc + gi) % 2
                gbank, ubank = 2 * pidx, 2 * pidx + 1
                for wsel, bank in ((0, gbank), (1, ubank)):
                    parts = []
                    for k in range(8):
                        parts.append((
                            lambda e, bank=bank, wsel=wsel, k=k, g=g, wgubf_ap=wgubf_ap: e.matmul(
                                pb[bank],
                                wgubf_ap[:, wsel * 1024 + 128 * k: wsel * 1024 + 128 * (k + 1)],
                                x1bf[:, k * TOK + g * TG: k * TOK + (g + 1) * TG],
                                start=(k == 0), stop=(k == 7)),
                            (r_wgubf[par], rx1g[g][k]), (r_pb[bank],)))
                    P.op("tensor", "pe", 1, *parts)
                sgap = v32(o_sg + 2048 * pidx, TG)
                P.op("scalar", "act", 1,
                     (lambda e, gbank=gbank, sgap=sgap: e.activation(
                         out=sgap, in_=pb[gbank], func=AF.Silu, bias=0.0, scale=S_DEQ),
                      (r_pb[gbank],), (r_sg[pidx],)))
                hsl = h[:, hc * 1024 + gi * TG: hc * 1024 + (gi + 1) * TG]
                P.op("vector", "dve", 1,
                     (lambda e, hsl=hsl, sgap=sgap, ubank=ubank: e.tensor_tensor(
                         out=hsl, in0=pb[ubank], in1=sgap, op=ALU.mult),
                      (r_pb[ubank], r_sg[pidx]), (r_h,)))
        for gi in range(2):
            g = 2 * hh + gi
            pg = g % 2
            ybase = y2off[pg]
            for oc in range(NCH):
                par = wd_slot[0] % 4
                wd_slot[0] += 1
                GQ.dma(lambda e, oc=oc, par=par: e.dma_start(
                           out=v16(o_wdbf + 6144 * par, 3072), in_=wd_d[oc]),
                       (), (r_wdbf[par],))
                wdbf_ap = v16(o_wdbf + 6144 * par, 3072)
                bank = dwn_rr[0] % 4
                dwn_rr[0] += 1
                parts = []
                for hc in range(HCN):
                    parts.append((
                        lambda e, bank=bank, hc=hc, gi=gi, wdbf_ap=wdbf_ap: e.matmul(
                            pb[bank], wdbf_ap[:, 128 * hc:128 * (hc + 1)],
                            h[:, hc * 1024 + gi * TG: hc * 1024 + (gi + 1) * TG],
                            start=(hc == 0), stop=(hc == HCN - 1)),
                        (r_wdbf[par], r_h), (r_pb[bank],)))
                P.op("tensor", "pe", 1, *parts)
                ds_ = dtmp_rr[0] % 4
                dtmp_rr[0] += 1
                dtm = v32(o_dtmp + 2048 * ds_, TG)
                P.op("scalar", "act", 1,
                     (lambda e, bank=bank, dtm=dtm: e.activation(
                         out=dtm, in_=pb[bank], func=AF.Identity, bias=0.0, scale=S_DEQ2),
                      (r_pb[bank],), (r_dtmp[ds_],)))
                step_ln2()
                step_ln2()
                y2sl = v32(ybase + oc * TG * 4, TG)
                x1sl = x1bf[:, oc * TOK + g * TG: oc * TOK + (g + 1) * TG]
                tt(y2sl, dtm, x1sl, ALU.add, [r_dtmp[ds_], rx1g[g][oc]],
                   [r_y2p[pg][oc // 2]])
                y2bfp = v16(o_ybf + 8192 * pg, NCH * TG)
                P.op("scalar", "act", 1,
                     (lambda e, y2sl=y2sl, y2bfp=y2bfp, oc=oc: e.activation(
                         out=y2bfp[:, oc * TG:(oc + 1) * TG], in_=y2sl,
                         func=AF.Identity, bias=0.0, scale=1.0),
                      (r_y2p[pg][oc // 2],), (r_ybfp[pg],)))
                tt(_sq_ap(oc), y2bfp[:, oc * TG:(oc + 1) * TG],
                   y2bfp[:, oc * TG:(oc + 1) * TG], ALU.mult, [r_ybfp[pg]],
                   [r_ysqc[oc]])
            pending_stages.extend(ln2_stages(g))

    flush_ln2()
    for (ssem, sval) in store_toks:
        P.raw_wait("sync", ssem, sval)

    with nc.Block() as block:
        for eng in _ENGS:
            def body(h_, eng=eng):
                for builder, waits, incspec in P.ops[eng]:
                    for s, v in waits:
                        h_.wait_ge(sems[s], v)
                    if builder is not None:
                        ins = builder(h_)
                        if incspec is not None:
                            ins.then_inc(sems[incspec[0]], incspec[1])
            getattr(block, eng)(body)

    return nc


def _pack_lhsT(w_t, n_out_chunks):
    d_in, d_out = w_t.shape
    k = d_in // 128
    a = w_t.reshape(k, 128, n_out_chunks, 128)
    return np.ascontiguousarray(a.transpose(2, 1, 0, 3)).reshape(n_out_chunks, 128, k * 128)


def _host_pack(inputs):
    w_qkv = np.asarray(inputs["L0_attn_W_qkv"], dtype=np.float32)
    b_qkv = np.asarray(inputs["L0_attn_b_qkv"], dtype=np.float32)
    w_out = np.asarray(inputs["L0_attn_W_out"], dtype=np.float32)
    b_out = np.asarray(inputs["L0_attn_b_out"], dtype=np.float32)
    wq_w = np.asarray(inputs["L0_attn_weight_q"], dtype=np.float32)
    wv_w = np.asarray(inputs["L0_attn_weight_v"], dtype=np.float32)
    w_gate = np.asarray(inputs["L0_ffn_w_gate"])
    w_up = np.asarray(inputs["L0_ffn_w_up"])
    w_down = np.asarray(inputs["L0_ffn_w_down"])

    wq = _pack_lhsT(np.ascontiguousarray(w_qkv[0:1024].T), NCH)
    wv = _pack_lhsT(np.ascontiguousarray(w_qkv[2048:3072].T), NCH)
    wo = _pack_lhsT(np.ascontiguousarray(w_out.T), NCH)
    def _to_i8(a):
        assert a.dtype == np.int32 and a.dtype.byteorder in ("=", "<")
        return np.ascontiguousarray(a.view(np.int8)[..., 0::4])

    wg_p = _pack_lhsT(np.ascontiguousarray(w_gate.T), HCN)
    wu_p = _pack_lhsT(np.ascontiguousarray(w_up.T), HCN)
    wgu = _to_i8(np.ascontiguousarray(np.stack([wg_p, wu_p], axis=1)))
    wd = _to_i8(_pack_lhsT(np.ascontiguousarray(w_down.T), NCH))
    wfeat = np.ascontiguousarray(np.concatenate(
        [wq_w.transpose(0, 2, 1).reshape(D, 3), wv_w.transpose(0, 2, 1).reshape(D, 3)],
        axis=1))
    biasp = np.ascontiguousarray(np.stack(
        [b_qkv[0:1024], b_qkv[2048:3072], b_out], axis=1))

    return {"wq": wq, "wv": wv, "wo": wo, "wgu": wgu, "wd": wd,
            "wfeat": wfeat, "biasp": biasp}


def kernel(**inputs):
    x = np.asarray(inputs["L0_attn_query"], dtype=np.float32)
    common = _host_pack(inputs)
    _BREV = [0, 4, 2, 6, 1, 5, 3, 7]
    xt = x.reshape(16384 // 8, 8, D)[:, _BREV, :].reshape(16384, D)
    shards = [np.ascontiguousarray(
        xt[i * TOK:(i + 1) * TOK].T.reshape(NCH, 128, TOK)
        .transpose(1, 0, 2).reshape(128, NCH * TOK)) for i in range(NCORES)]

    nc = _build_nc()
    in_maps = [dict(common, x=shards[i]) for i in range(NCORES)]
    res = run_bass_kernel_spmd(nc, in_maps, core_ids=list(range(NCORES)))

    out = np.concatenate([np.asarray(res.results[i]["y"]).T for i in range(NCORES)], axis=0)
    out = out.reshape(16384 // 8, 8, D)[:, _BREV, :]
    return np.ascontiguousarray(out.reshape(2, 8192, D).astype(np.float32))
